# version 01 (final)
# baseline (speedup 1.0000x reference)
import contextlib
import numpy as np
import ml_dtypes
import concourse.bass as bass
import concourse.mybir as mybir
from concourse.bass_utils import run_bass_kernel_spmd

F32 = mybir.dt.float32
BF16 = mybir.dt.bfloat16
AF = mybir.ActivationFunctionType
ALU = mybir.AluOpType
AX = mybir.AxisListType

L, E, NH, DH, FFD, HHD = 4096, 1024, 16, 64, 4096, 5632
NEXP, NLAYER = 4, 6
NT = L // 128
NSL = L // 512
NCH = L // 1024
C_SHIFT = 12.0
EPS = 1e-5

STAGE = 4
FF2_NFC = 32
DO_W, DO_P, DO_S, DO_F = True, True, True, True
DBG_OUT = False
DO_W2 = True
W1MODE = 3

ENGS = ("tensor", "scalar", "vector", "gpsimd", "sync")


class Sem:
    def __init__(self, h, name):
        self.h, self.name, self.n = h, name, 0


class DmaChan:
    def __init__(self, sems):
        self.sems = sems
        self.i = 0

    def next(self):
        s = self.sems[self.i]
        self.i = (self.i + 1) % len(self.sems)
        return s

    def all(self):
        return [(s, s.n) for s in self.sems]


class Rec:
    def __init__(self):
        self.ops = {e: [] for e in ENGS}
        self.waited = {e: {} for e in ENGS}

    def emit(self, eng, build, waits=(), inc=None, k=1):
        wl = []
        flat = []
        for wv in waits:
            if wv is None:
                continue
            if isinstance(wv, list):
                flat.extend(x for x in wv if x is not None)
            else:
                flat.append(wv)
        for wv in flat:
            s, v = wv
            if v is None or v <= 0:
                continue
            if self.waited[eng].get(s.name, 0) >= v:
                continue
            self.waited[eng][s.name] = v
            wl.append((s.h, v))
        tgt, ih = None, None
        if inc is not None:
            inc.n += k
            tgt, ih = inc.n, inc.h
        def f(EE, wl=wl, build=build, ih=ih, k=k):
            for h, v in wl:
                EE.wait_ge(h, v)
            ins = build(EE)
            if ih is not None:
                ins.then_inc(ih, k)
        self.ops[eng].append(f)
        return tgt


def build_program(stage):
    NT = L // 128
    NSL = L // 512
    NCH = L // 1024
    nc = bass.Bass("TRN2", target_bir_lowering=False)

    x_in = nc.dram_tensor("x", [L, E], F32, kind="ExternalInput").ap()
    aWq = nc.dram_tensor("aWq", [NLAYER, E, E], F32, kind="ExternalInput").ap()
    aWk = nc.dram_tensor("aWk", [NLAYER, E, E], F32, kind="ExternalInput").ap()
    abq = nc.dram_tensor("abq", [NLAYER, E], F32, kind="ExternalInput").ap()
    abk = nc.dram_tensor("abk", [NLAYER, E], F32, kind="ExternalInput").ap()
    aln = {nm: nc.dram_tensor("a" + nm, [NLAYER, E], F32, kind="ExternalInput").ap()
           for nm in ("ln1_w", "ln1_b", "ln2_w", "ln2_b", "ln3_w", "ln3_b")}
    aW1 = nc.dram_tensor("aW1", [NLAYER, FFD, E], F32, kind="ExternalInput").ap()
    ab1 = nc.dram_tensor("ab1", [NLAYER, FFD], F32, kind="ExternalInput").ap()
    aW2 = nc.dram_tensor("aW2", [NLAYER, E, FFD], F32, kind="ExternalInput").ap()
    ab2 = nc.dram_tensor("ab2", [NLAYER, E], F32, kind="ExternalInput").ap()
    bW = {nm: nc.dram_tensor("bW" + nm, [E, E], F32, kind="ExternalInput").ap()
          for nm in ("q", "k", "v", "o")}
    bb = {nm: nc.dram_tensor("bb" + nm, [E], F32, kind="ExternalInput").ap()
          for nm in ("q", "k", "v", "o")}
    gate_w = nc.dram_tensor("gate_w", [8, E], F32, kind="ExternalInput").ap()
    mw1 = nc.dram_tensor("mw1", [8, E, HHD], F32, kind="ExternalInput").ap()
    mw2 = nc.dram_tensor("mw2", [8, HHD, E], F32, kind="ExternalInput").ap()
    mw3 = nc.dram_tensor("mw3", [8, E, HHD], F32, kind="ExternalInput").ap()
    out = nc.dram_tensor("out", [L, E], F32, kind="ExternalOutput").ap()

    dbgkind = "ExternalOutput" if DBG_OUT else "Internal"
    x_a = nc.dram_tensor("x_scr_a", [NT, 128, E], F32, kind=dbgkind).ap()
    x_b = nc.dram_tensor("x_scr_b", [NT, 128, E], F32, kind="Internal").ap()
    m2rel = nc.dram_tensor("m2rel", [128, 2 * L + 128], BF16, kind="Internal").ap()
    qt_hbm = nc.dram_tensor("qt_scr", [NSL, 8, 128, 512], BF16, kind=dbgkind).ap()
    h2_hbm = nc.dram_tensor("h2_scr", [NT, 128, E], BF16, kind=dbgkind).ap()
    w1t_hbm = nc.dram_tensor("w1t_scr", [NLAYER, 32, 128, E], BF16, kind=dbgkind).ap()
    w2t_hbm = nc.dram_tensor("w2t_scr", [NLAYER, 32, 128, E], BF16, kind=dbgkind).ap()
    at_hbm = nc.dram_tensor("at_scr", [8, NT, 128, 128], BF16, kind="Internal").ap()
    y_hbm = nc.dram_tensor("y_scr", [8, 128, L], F32, kind=dbgkind).ap()
    dbgA = nc.dram_tensor("dbgA", [L // 128, 128, E], F32, kind=dbgkind).ap()
    dbgB = nc.dram_tensor("dbgB", [32, 128, 512], BF16, kind=dbgkind).ap()
    dbgC = nc.dram_tensor("dbgC", [128, 4096], BF16, kind=dbgkind).ap()

    R = Rec()

    with contextlib.ExitStack() as ES:
        def mksem(name):
            return Sem(ES.enter_context(nc.semaphore(name)), name)
        sPE = mksem("sPE")
        sACT = mksem("sACT")
        sDVE = mksem("sDVE")
        sGP = mksem("sGP")
        sIN = DmaChan([mksem("sIN%d" % i) for i in range(8)])
        sW1 = DmaChan([mksem("sW1_%d" % i) for i in range(8)])
        sWG = DmaChan([mksem("sWG%d" % i) for i in range(8)])
        sST = DmaChan([mksem("sST%d" % i) for i in range(8)])
        sSY = DmaChan([mksem("sSY%d" % i) for i in range(8)])

        def sb(name, shape, dtype):
            return ES.enter_context(nc.sbuf_tensor(name, shape, dtype))

        BIG1 = sb("BIG1", [128, 32768], BF16)
        BIG2 = sb("BIG2", [128, 32768], BF16)
        UNA = sb("UNA", [128, 2048], F32)
        UNA_BF = UNA[:].bitcast(BF16)
        LNBC = sb("LNBC", [128, 4096], BF16)
        NTR = sb("NTR", [128, 2048], BF16)
        consts = sb("consts", [128, 4], F32)
        stat = sb("stat", [128, 16], F32)
        zp = sb("zp", [128, 8], F32)
        ident = sb("ident", [128, 128], BF16)
        identf = sb("identf", [128, 128], F32)
        tri = sb("tri", [128, 128], BF16)
        iota_t = sb("iota_t", [128, 128], F32)
        ones_r = sb("ones_r", [1, 128], BF16)
        bqT = sb("bqT", [128, 8], F32)
        bkT = sb("bkT", [128, 8], F32)
        b1T = sb("b1T", [128, 32], F32)
        row8 = sb("row8", [32, 128], F32)
        jnk = sb("jnk", [128, 1024], BF16)

        def h_t(i):
            return BIG1[:, i * E:(i + 1) * E]
        def kt_c(dc):
            return BIG2[:, dc * L:(dc + 1) * L]

        def dma(eng, chan, dst, src, waits=(), accum=False):
            s = chan.next()
            if accum:
                v = R.emit(eng, lambda EE, d=dst, s2=src:
                           EE.dma_start(out=d, in_=s2, accum_op=ALU.add),
                           waits=waits, inc=s, k=16)
            else:
                v = R.emit(eng, lambda EE, d=dst, s2=src: EE.dma_start(out=d, in_=s2),
                           waits=waits, inc=s, k=16)
            return (s, v)

        def mm(o, lhsT, rhs, start, stop, waits=(), inc=None):
            return R.emit("tensor",
                          lambda EE, oo=o, a=lhsT, b=rhs, st=start, sp=stop:
                          EE.matmul(oo, a, b, start=st, stop=sp),
                          waits=waits, inc=inc)

        def tr(o, i, idn, waits=(), inc=None):
            return R.emit("tensor", lambda EE, oo=o, ii=i, dd=idn:
                          EE.transpose(oo, ii, dd), waits=waits, inc=inc)

        def act(o, i, func, bias=0.0, scale=1.0, accum=None, waits=(), inc=None):
            return R.emit("scalar", lambda EE, oo=o, ii=i, ff=func, bb=bias, ss=scale,
                          aa=accum: EE.activation(out=oo, in_=ii, func=ff, bias=bb,
                                                  scale=ss, accum_out=aa),
                          waits=waits, inc=inc)

        def vv(name, *args, waits=(), inc=None, **kw):
            return R.emit("vector", lambda EE, n=name, a=args, k2=kw:
                          getattr(EE, n)(*a, **k2), waits=waits, inc=inc)

        def gg(name, *args, waits=(), inc=None, **kw):
            return R.emit("gpsimd", lambda EE, n=name, a=args, k2=kw:
                          getattr(EE, n)(*a, **k2), waits=waits, inc=inc)

        def wait_only(eng, waits):
            R.emit(eng, lambda EE: None, waits=waits)

        def fence(engs=ENGS):
            toks = [(sPE, sPE.n), (sACT, sACT.n), (sDVE, sDVE.n)]
            for e in engs:
                wait_only(e, toks)

        def ln_stats(t_ap, width, waits=()):
            v1 = vv("reduce_sum", stat[:, 0:1], t_ap, axis=AX.X, waits=waits, inc=sDVE)
            a1 = act(jnk[:, 0:width], t_ap, AF.Square, accum=stat[:, 1:2],
                     waits=waits, inc=sACT)
            v2 = vv("tensor_scalar", stat[:, 2:3], stat[:, 0:1], 1.0 / width, None,
                    ALU.mult, waits=[(sDVE, v1)], inc=sDVE)
            v3 = vv("tensor_scalar", stat[:, 3:4], stat[:, 1:2], 1.0 / width, None,
                    ALU.mult, waits=[(sACT, a1), (sDVE, v2)], inc=sDVE)
            v4 = vv("tensor_tensor", stat[:, 6:7], stat[:, 2:3], stat[:, 2:3],
                    ALU.mult, waits=[(sDVE, v3)], inc=sDVE)
            v5 = vv("tensor_tensor", stat[:, 3:4], stat[:, 3:4], stat[:, 6:7],
                    ALU.subtract, waits=[(sDVE, v4)], inc=sDVE)
            a2 = act(stat[:, 3:4], stat[:, 3:4], AF.Ln, bias=consts[:, 1:2],
                     waits=[(sDVE, v5)], inc=sACT)
            a3 = act(stat[:, 4:5], stat[:, 3:4], AF.Exp, scale=-0.5,
                     waits=[(sACT, a2)], inc=sACT)
            v6 = vv("tensor_scalar", stat[:, 5:6], stat[:, 2:3], stat[:, 4:5], -1.0,
                    ALU.mult, ALU.mult, waits=[(sACT, a3)], inc=sDVE)
            return v6, a3

        def ln_norm(dst, src, waits=()):
            return act(dst, src, AF.Identity, bias=stat[:, 5:6], scale=stat[:, 4:5],
                       waits=list(waits), inc=sACT)

        def bcast_row(dst, row_ap, row2_ap, width, ps, waits=(), pe_waits=()):
            hop = vv("tensor_copy", row2_ap[0:1, 0:width], row_ap[0:1, 0:width],
                     waits=list(waits), inc=sDVE)
            for hh in range(0, width, 512):
                w = min(512, width - hh)
                t = mm(ps[:, hh:hh + w], ones_r[0:1, :], row2_ap[0:1, hh:hh + w],
                       True, True,
                       waits=(list(pe_waits) + [(sDVE, hop)]) if hh == 0 else [],
                       inc=sPE)
            ev = vv("tensor_copy", dst, ps[:, 0:width], waits=[(sPE, sPE.n)], inc=sDVE)
            return ev

        vv("memset", consts[:, 0:1], -C_SHIFT, inc=sDVE)
        vv("memset", consts[:, 1:2], EPS, inc=sDVE)
        vv("memset", consts[:, 2:3], 1.0, inc=sDVE)
        vv("memset", consts[:, 3:4], 0.0, inc=sDVE)
        vv("memset", ones_r[:, :], 1.0, inc=sDVE)
        cdone = sDVE.n
        act(stat[:, 0:1], consts[:, 2:3], AF.Exp, waits=[(sDVE, cdone)], inc=sACT)
        tbl_tok = act(stat[:, 0:1], consts[:, 2:3], AF.Ln, inc=sACT)
        g1 = gg("iota", iota_t[:], pattern=[[1, 128]], base=0, channel_multiplier=-1,
                allow_small_or_imprecise_dtypes=True, inc=sGP)
        vv("tensor_single_scalar", ident[:], iota_t[:], 0.0, ALU.is_equal,
           waits=[(sGP, g1)], inc=sDVE)
        vv("tensor_single_scalar", identf[:], iota_t[:], 0.0, ALU.is_equal, inc=sDVE)
        id_tok = vv("tensor_single_scalar", tri[:], iota_t[:], 0.0, ALU.is_ge, inc=sDVE)

        with contextlib.ExitStack() as MS:
            m2i = MS.enter_context(nc.sbuf_tensor("m2i", [128, 2 * L + 128], F32))
            m2o = MS.enter_context(nc.sbuf_tensor("m2o", [128, 2 * L + 128], BF16))
            g2 = gg("iota", m2i[:], pattern=[[1, 2 * L + 128]], base=-(L - 1),
                    channel_multiplier=-1, allow_small_or_imprecise_dtypes=True,
                    inc=sGP)
            act(m2i[:], m2i[:], AF.Abs, waits=[(sGP, g2), (sACT, tbl_tok)], inc=sACT)
            act(m2i[:], m2i[:], AF.Exp, scale=-1.0 / 16.0, inc=sACT)
            a3 = act(m2o[:], m2i[:], AF.Exp, inc=sACT)
            rel_st = dma("sync", sST, m2rel[:, :], m2o[:], waits=[(sACT, a3)])
        wait_only("vector", [rel_st])

        with contextlib.ExitStack() as PS:
            wstg = PS.enter_context(nc.sbuf_tensor("wstg", [128, 2 * FFD], BF16))
            wev = PS.enter_context(nc.sbuf_tensor("wev", [128, 2 * FFD], BF16))
            pstr = PS.enter_context(nc.psum_tensor("pstr", [128, 2048], BF16))
            stg_free = {0: rel_st, 1: rel_st}
            ev_free = {0: rel_st, 1: rel_st}
            bank_free = {0: None, 1: None}
            slot_ev = {}
            for layer in range(NLAYER if DO_W else 0):
                for fc in range(32):
                    par = fc % 2
                    stg = wstg[:, par * E:(par + 1) * E]
                    ev = wev[:, par * E:(par + 1) * E]
                    ld = dma("gpsimd", sWG, stg, aW1[layer, fc * 128:(fc + 1) * 128, :],
                             waits=[stg_free[par]])
                    if W1MODE < 2:
                        stg_free[par] = ld
                        continue
                    bank = pstr[:, par * 1024:(par + 1) * 1024]
                    t = None
                    for ec in range(8):
                        w = []
                        if ec == 0:
                            w = [ld, (sDVE, id_tok), bank_free.get(par),
                                 ev_free[par]]
                        t = tr(bank[:, ec * 128:(ec + 1) * 128],
                               stg[:, ec * 128:(ec + 1) * 128], ident[:],
                               waits=w, inc=(sPE if ec == 7 else None))
                    e = vv("tensor_copy", ev, bank, waits=[(sPE, t)], inc=sDVE)
                    bank_free[par] = (sDVE, e)
                    stg_free[par] = (sPE, sPE.n)
                    if W1MODE < 3:
                        ev_free[par] = (sDVE, e)
                        continue
                    st = dma("sync", sST, w1t_hbm[layer, fc], ev,
                             waits=[(sDVE, e)])
                    ev_free[par] = st
                stg_free = {0: (sPE, sPE.n), 1: (sPE, sPE.n)}
                ev_free = {0: sST.all(), 1: sST.all()}
                for ec in range(8 if DO_W2 else 0):
                    par = ec % 2
                    stg = wstg[:, par * FFD:(par + 1) * FFD]
                    ev = wev[:, par * FFD:(par + 1) * FFD]
                    ld = dma("gpsimd", sWG, stg, aW2[layer, ec * 128:(ec + 1) * 128, :],
                             waits=[stg_free[par]])
                    for g in range(4):
                        gb = g % 2
                        bank = pstr[:, gb * 1024:(gb + 1) * 1024]
                        t = None
                        for j in range(8):
                            fc = g * 8 + j
                            w = []
                            if j == 0:
                                w = [bank_free.get(gb)]
                                if g == 0:
                                    w += [ld, ev_free[par]]
                            t = tr(bank[:, j * 128:(j + 1) * 128],
                                   stg[:, fc * 128:(fc + 1) * 128], ident[:],
                                   waits=w, inc=(sPE if j == 7 else None))
                        e = vv("tensor_copy", ev[:, g * 1024:(g + 1) * 1024], bank,
                               waits=[(sPE, t)], inc=sDVE)
                        bank_free[gb] = (sDVE, e)
                    stg_free[par] = (sPE, sPE.n)
                    st = dma("sync", sST,
                             w2t_hbm[layer, :, :, ec * 128:(ec + 1) * 128]
                             .rearrange("c p j -> p c j"),
                             ev.rearrange("p (c j) -> p c j", j=128),
                             waits=[(sDVE, sDVE.n)])
                    ev_free[par] = st
                stg_free = {0: (sPE, sPE.n), 1: (sPE, sPE.n)}
                ev_free = {0: sST.all(), 1: sST.all()}
            w2t_done = {layer: None for layer in range(NLAYER)}
        wt_all_done = sST.all()
        fence()

        def xs_t(par):
            return UNA[:, par * E:(par + 1) * E]

        b0_free = {0: None, 1: None}
        for i in range(NT):
            par = i % 2
            ld = dma("sync", sIN, xs_t(par), x_in[i * 128:(i + 1) * 128, :],
                     waits=[b0_free[par]])
            v6, a3 = ln_stats(xs_t(par), E, waits=[ld])
            u = ln_norm(h_t(i), xs_t(par), waits=[(sDVE, v6), (sACT, a3)])
            b0_free[par] = (sACT, u)
        fence()

        t1_f = UNA[:, 0:1024]
        h1_b = UNA_BF[:, 2048:3072]
        h2_b = UNA_BF[:, 3072:4096]

        kt_ready = None
        kt_consumed = None
        h_ready_tok = [(sACT, sACT.n)] * NT

        for layer in range(NLAYER):
            if not DO_P:
                break
            with contextlib.ExitStack() as PS:
                ps_qk = PS.enter_context(nc.psum_tensor("ps_qk_%d" % layer, [128, 1024], F32))
                ps_tp = PS.enter_context(nc.psum_tensor("ps_tp_%d" % layer, [128, 4096], BF16))
                ps_bc = PS.enter_context(nc.psum_tensor("ps_bc_%d" % layer, [128, 1024], F32))
                wqT = PS.enter_context(nc.sbuf_tensor("wqT_%d" % layer, [128, 8192], BF16))
                wkT = PS.enter_context(nc.sbuf_tensor("wkT_%d" % layer, [128, 8192], BF16))
                hTs = PS.enter_context(nc.sbuf_tensor("hTs_%d" % layer, [128, 4096], BF16))
                qtr = PS.enter_context(nc.sbuf_tensor("qtr_%d" % layer, [128, 1024], BF16))
                rowbf = PS.enter_context(nc.sbuf_tensor("rb_%d" % layer, [1, E], BF16))
                rowbf2 = PS.enter_context(nc.sbuf_tensor("rb2_%d" % layer, [1, E], BF16))
                wstag = PS.enter_context(nc.sbuf_tensor("ws_%d" % layer, [128, 1024], BF16))

                ld = dma("sync", sIN, row8[0:8, :],
                         abq[layer].rearrange("(c j) -> c j", j=128))
                t = tr(ps_bc[:, 0:8], row8[0:8, :], identf[0:8, 0:8],
                       waits=[ld], inc=sPE)
                bq_tok = vv("tensor_copy", bqT[:, :], ps_bc[:, 0:8],
                            waits=[(sPE, t)], inc=sDVE)
                ld = dma("sync", sIN, row8[0:8, :],
                         abk[layer].rearrange("(c j) -> c j", j=128),
                         waits=[(sPE, t)])
                t = tr(ps_bc[:, 8:16], row8[0:8, :], identf[0:8, 0:8],
                       waits=[ld, (sDVE, bq_tok)], inc=sPE)
                bk_tok = vv("tensor_copy", bkT[:, :], ps_bc[:, 8:16],
                            waits=[(sPE, t)], inc=sDVE)
                ld = dma("sync", sIN, row8[0:32, :],
                         ab1[layer].rearrange("(c j) -> c j", j=128),
                         waits=[(sPE, t)])
                t = tr(ps_bc[:, 32:64], row8[0:32, :], identf[0:32, 0:32],
                       waits=[ld, (sDVE, bk_tok)], inc=sPE)
                b1_tok = vv("tensor_copy", b1T[:, :], ps_bc[:, 32:64],
                            waits=[(sPE, t)], inc=sDVE)
                lnb_tok = {-1: (sDVE, b1_tok)}
                for si, nm in enumerate(("ln1_w", "ln1_b", "ln2_w", "ln2_b")):
                    ld = dma("gpsimd", sWG, rowbf[0:1, 0:E], aln[nm][layer][None, :],
                             waits=[lnb_tok.get(si - 1)])
                    ev = bcast_row(LNBC[:, si * E:(si + 1) * E], rowbf, rowbf2, E,
                                   ps_bc, waits=[ld, lnb_tok[si - 1]],
                                   pe_waits=[lnb_tok[si - 1]])
                    lnb_tok[si] = (sDVE, ev)
                lnS_tok = lnb_tok[3]

                wstg_free = None
                wq_tok = wk_tok = None
                tp_bank_free = {}
                for which, (wsrc, wdst) in enumerate(((aWq, wqT), (aWk, wkT))):
                    for dc in range(8):
                        gb = dc % 2
                        ld = dma("gpsimd", sWG, wstag[:, :],
                                 wsrc[layer, dc * 128:(dc + 1) * 128, :],
                                 waits=[wstg_free])
                        bank = ps_tp[:, gb * 1024:(gb + 1) * 1024]
                        t = None
                        for ec in range(8):
                            w = []
                            if ec == 0:
                                w = [ld, (sDVE, id_tok), tp_bank_free.get(gb)]
                            t = tr(bank[:, ec * 128:(ec + 1) * 128],
                                   wstag[:, ec * 128:(ec + 1) * 128],
                                   ident[:], waits=w, inc=(sPE if ec == 7 else None))
                        e = vv("tensor_copy",
                               wdst[:].rearrange("p (c r) -> p c r", r=1024)
                               [:, :, dc * 128:(dc + 1) * 128],
                               bank[:].rearrange("p (c r) -> p c r", r=128),
                               waits=[(sPE, t)], inc=sDVE)
                        tp_bank_free[gb] = (sDVE, e)
                        wstg_free = (sPE, sPE.n)
                    if which == 0:
                        wq_tok = (sDVE, sDVE.n)
                    else:
                        wk_tok = (sDVE, sDVE.n)

                qt_store = [None] * NSL
                qtr_free = {0: None, 1: None}
                kt_ev_free = {0: None, 1: None}
                for s in range(NSL):
                    for k in range(4):
                        lt = s * 4 + k
                        gb = 2 + k % 2
                        bank = ps_tp[:, gb * 1024:(gb + 1) * 1024]
                        t = None
                        for ec in range(8):
                            w = []
                            if ec == 0:
                                w = [h_ready_tok[lt], tp_bank_free.get(gb)]
                                if layer > 0 and s == 0 and k == 0:
                                    w.append(kt_consumed)
                            t = tr(bank[:, ec * 128:(ec + 1) * 128],
                                   h_t(lt)[:, ec * 128:(ec + 1) * 128], ident[:],
                                   waits=w, inc=(sPE if ec == 7 else None))
                        e = vv("tensor_copy",
                               hTs[:].rearrange("p (c r) -> p c r", r=512)
                               [:, :, k * 128:(k + 1) * 128],
                               bank[:].rearrange("p (c r) -> p c r", r=128),
                               waits=[(sPE, t)], inc=sDVE)
                        tp_bank_free[gb] = (sDVE, e)
                    hts_tok = (sDVE, sDVE.n)
                    for dc in range(8):
                        par = dc % 2
                        pk = ps_qk[:, par * 512:(par + 1) * 512]
                        for ec in range(8):
                            mm(pk, wkT[:, ec * 1024 + dc * 128:ec * 1024 + (dc + 1) * 128],
                               hTs[:, ec * 512:(ec + 1) * 512],
                               ec == 0, ec == 7,
                               waits=([hts_tok, wk_tok, kt_ev_free[par]]
                                      if ec == 0 else []),
                               inc=(sPE if ec == 7 else None))
                        e = act(kt_c(dc)[:, s * 512:(s + 1) * 512], pk, AF.Identity,
                                bias=bkT[:, dc:dc + 1],
                                waits=[(sPE, sPE.n)], inc=sACT)
                        kt_ev_free[par] = (sACT, e)
                    for dc in range(8):
                        par = dc % 2
                        pk = ps_qk[:, par * 512:(par + 1) * 512]
                        for ec in range(8):
                            mm(pk, wqT[:, ec * 1024 + dc * 128:ec * 1024 + (dc + 1) * 128],
                               hTs[:, ec * 512:(ec + 1) * 512],
                               ec == 0, ec == 7,
                               waits=([wq_tok, kt_ev_free[par], qtr_free[par]]
                                      if ec == 0 else []),
                               inc=(sPE if ec == 7 else None))
                        e = act(qtr[:, par * 512:(par + 1) * 512], pk, AF.Identity,
                                bias=bqT[:, dc:dc + 1],
                                waits=[(sPE, sPE.n)], inc=sACT)
                        st = dma("sync", sST, qt_hbm[s, dc],
                                 qtr[:, par * 512:(par + 1) * 512],
                                 waits=[(sACT, e)])
                        qtr_free[par] = st
                    qt_store[s] = sST.all()
                kt_ready = (sACT, sACT.n)
            fence()

            if not DO_S:
                continue
            with contextlib.ExitStack() as PS:
                WAVG = PS.enter_context(nc.sbuf_tensor("WAVG_%d" % layer, [128, L], F32))
                ptbuf = PS.enter_context(nc.sbuf_tensor("ptb_%d" % layer, [128, 2 * L], BF16))
                pt_b = [ptbuf[:, 0:L], ptbuf[:, L:2 * L]]
                rel_sb = PS.enter_context(nc.sbuf_tensor("rel_%d" % layer, [128, L], BF16))[:]
                qt_slab = PS.enter_context(nc.sbuf_tensor("qts_%d" % layer, [128, 4096], BF16))[:]
                psS = [PS.enter_context(nc.psum_tensor("psS%d_%d" % (i, layer),
                                                       [128, 1024], F32))
                       for i in range(2)]
                ps_sa = PS.enter_context(nc.psum_tensor("ps_sa_%d" % layer,
                                                        [128, 1024], F32))
                ps_nt = PS.enter_context(nc.psum_tensor("ps_nt_%d" % layer,
                                                        [128, 2048], BF16))
                psS_free = [None, None]
                pt_free = [None, None]
                nt_bank_free = {}
                ntr_free = {}
                sa_guard = None
                qt_tok = None
                rel_tok = None
                rel_free = None
                h2_free = None
                for lt in range(NT):
                    s, k4 = lt // 4, lt % 4
                    if k4 == 0:
                        qt_tok = dma("sync", sIN,
                                     qt_slab.rearrange("p (c u) -> p c u", u=512),
                                     qt_hbm[s].rearrange("c p u -> p c u"),
                                     waits=[qt_store[s],
                                            (sPE, sPE.n) if s > 0 else None])
                    rel_tok = dma("sync", sIN, rel_sb,
                                  m2rel[:, (L - 1) - 128 * lt: (L - 1) - 128 * lt + L],
                                  waits=[rel_st, rel_free])
                    pp = lt % 2
                    for h in range(NH):
                        dc, half = h // 2, h % 2
                        pp = h % 2
                        qrow = half * 64
                        for c in range(NCH):
                            par = c % 2
                            for cc in range(2):
                                w = []
                                if cc == 0:
                                    w = [psS_free[par]]
                                    if h == 0 and c == 0:
                                        w += [qt_tok, kt_ready]
                                mm(psS[par][:, cc * 512:(cc + 1) * 512],
                                   qt_slab[qrow:qrow + 64,
                                           dc * 512 + k4 * 128:dc * 512 + (k4 + 1) * 128],
                                   kt_c(dc)[qrow:qrow + 64,
                                            c * 1024 + cc * 512:c * 1024 + (cc + 1) * 512],
                                   True, True, waits=w,
                                   inc=(sPE if cc == 1 else None))
                            e = act(pt_b[pp][:, c * 1024:(c + 1) * 1024],
                                    psS[par][:, 0:1024], AF.Exp,
                                    bias=consts[:, 0:1], scale=0.125,
                                    accum=zp[:, pp * NCH + c:pp * NCH + c + 1],
                                    waits=[(sPE, sPE.n)] + ([pt_free[pp]] if c == 0 else []),
                                    inc=sACT)
                            psS_free[par] = (sACT, e)
                        zc = 8 + pp
                        zs = vv("reduce_sum", stat[:, zc:zc + 1],
                                zp[:, pp * NCH:pp * NCH + NCH],
                                axis=AX.X, waits=[(sACT, sACT.n)], inc=sDVE)
                        rc_ = 10 + pp
                        rcp = vv("reciprocal", stat[:, rc_:rc_ + 1],
                                 stat[:, zc:zc + 1], waits=[(sDVE, zs)], inc=sDVE)
                        if h == 0:
                            cb = vv("tensor_scalar", WAVG[:, :], pt_b[pp][:, :],
                                    stat[:, rc_:rc_ + 1], None, ALU.mult,
                                    waits=[(sDVE, rcp)], inc=sDVE)
                        else:
                            cb = vv("scalar_tensor_tensor", WAVG[:, :], pt_b[pp][:, :],
                                    stat[:, rc_:rc_ + 1], WAVG[:, :], ALU.mult,
                                    ALU.add, waits=[(sDVE, rcp)], inc=sDVE)
                        pt_free[pp] = (sDVE, cb)
                    ew = act(pt_b[0][:, :], WAVG[:, :], AF.Exp, scale=1.0 / 16.0,
                             waits=[(sDVE, sDVE.n)], inc=sACT)
                    nt = vv("scalar_tensor_tensor", pt_b[1][:, :], pt_b[0][:, :],
                            0.0, rel_sb, ALU.bypass, ALU.mult,
                            accum_out=stat[:, 13:14],
                            waits=[(sACT, ew), rel_tok], inc=sDVE)
                    rel_free = (sDVE, nt)
                    pt_free[0] = (sDVE, nt)
                    vv("reciprocal", stat[:, 14:15], stat[:, 13:14],
                       waits=[(sDVE, nt)], inc=sDVE)
                    r2_tok = sDVE.n
                    sa_done = None
                    ngrp = NT // 8
                    for g in range(ngrp):
                        gb = g % 2
                        bank = ps_nt[:, gb * 1024:(gb + 1) * 1024]
                        t = None
                        for j in range(8):
                            mc = g * 8 + j
                            w = []
                            if j == 0:
                                w = [(sDVE, nt), nt_bank_free.get(gb),
                                     sa_guard]
                            t = tr(bank[:, j * 128:(j + 1) * 128],
                                   pt_b[1][:, mc * 128:(mc + 1) * 128], ident[:],
                                   waits=w, inc=(sPE if j == 7 else None))
                        rs = g % 2
                        e = vv("tensor_copy", NTR[:, rs * 1024:(rs + 1) * 1024], bank,
                               waits=[(sPE, t), ntr_free.get(rs)], inc=sDVE)
                        nt_bank_free[gb] = (sDVE, e)
                        for j in range(8):
                            mc = g * 8 + j
                            for cc in range(2):
                                sa_done = mm(
                                    ps_sa[:, cc * 512:(cc + 1) * 512],
                                    NTR[:, rs * 1024 + j * 128:rs * 1024 + (j + 1) * 128],
                                    h_t(mc)[:, cc * 512:(cc + 1) * 512],
                                    mc == 0, mc == NT - 1,
                                    waits=[(sDVE, e)] if (j == 0 and cc == 0) else [],
                                    inc=(sPE if (mc == NT - 1 and cc == 1) else None))
                        ntr_free[rs] = (sPE, sPE.n)
                    pt_free[1] = (sDVE, sDVE.n)
                    t1tok = vv("scalar_tensor_tensor", t1_f[:, :], ps_sa[:, :],
                               stat[:, 14:15], h_t(lt), ALU.mult, ALU.add,
                               waits=[(sPE, sa_done), (sDVE, r2_tok),
                                      (sACT, sACT.n)], inc=sDVE)
                    sa_guard = (sDVE, t1tok)
                    v6, a3 = ln_stats(t1_f[:, :], E, waits=[(sDVE, sDVE.n)])
                    u = ln_norm(h1_b, t1_f[:, :], waits=[(sDVE, v6), (sACT, a3), lnS_tok])
                    vv("tensor_tensor", h1_b, h1_b, LNBC[:, 0:E], ALU.mult,
                       waits=[(sACT, u)], inc=sDVE)
                    vv("tensor_tensor", h1_b, h1_b, LNBC[:, E:2 * E], ALU.add,
                       inc=sDVE)
                    v6, a3 = ln_stats(h1_b, E, waits=[(sDVE, sDVE.n)])
                    u = ln_norm(h2_b, h1_b, waits=[(sDVE, v6), (sACT, a3), h2_free])
                    vv("tensor_tensor", h2_b, h2_b, LNBC[:, 2 * E:3 * E], ALU.mult,
                       waits=[(sACT, u)], inc=sDVE)
                    hv = vv("tensor_tensor", h2_b, h2_b, LNBC[:, 3 * E:4 * E], ALU.add,
                            inc=sDVE)
                    st = dma("sync", sST, h2_hbm[lt], h2_b, waits=[(sDVE, hv)])
                    h2_store = st
                    h2_free = h2_store
                    if lt == 0:
                        h2_first_store = st
                    h2_store_tok = st
                kt_consumed = (sPE, sPE.n)
                h2_stores = sST.all()
            fence()

            if not DO_F:
                continue
            with contextlib.ExitStack() as PS:
                ps_f1 = PS.enter_context(nc.psum_tensor("ps_f1_%d" % layer,
                                                        [128, 1024], F32))
                ps_tf = PS.enter_context(nc.psum_tensor("ps_tf_%d" % layer,
                                                        [128, 2048], BF16))
                ps_y = [PS.enter_context(nc.psum_tensor("ps_y%d_%d" % (i, layer),
                                                        [128, 1024], F32))
                        for i in range(2)]
                wroll = PS.enter_context(nc.sbuf_tensor("wr_%d" % layer, [128, 16384], BF16))
                rowbf = PS.enter_context(nc.sbuf_tensor("rbf_%d" % layer, [1, E], BF16))
                rowbf2 = PS.enter_context(nc.sbuf_tensor("rbf2_%d" % layer, [1, E], BF16))
                def r1_t(ft):
                    return BIG2[:, ft * 512:(ft + 1) * 512]
                def h2T_c(ec):
                    return BIG2[:, 16384 + ec * 512:16384 + (ec + 1) * 512]
                def h2s_t(k):
                    return BIG2[:, 20480 + k * E:20480 + (k + 1) * E]
                lnb_tok = (sDVE, sDVE.n)
                for si, srcrow in enumerate((aln["ln3_w"][layer], aln["ln3_b"][layer],
                                             ab2[layer])):
                    ld = dma("gpsimd", sWG, rowbf[0:1, 0:E], srcrow[None, :],
                             waits=[lnb_tok])
                    ev = bcast_row(LNBC[:, si * E:(si + 1) * E], rowbf, rowbf2, E,
                                   ps_f1, waits=[ld, lnb_tok],
                                   pe_waits=[lnb_tok, (sACT, sACT.n)])
                    lnb_tok = (sDVE, ev)
                lnF_tok = lnb_tok
                w1r_free = [None] * 4
                w2s_free = [None] * 4
                w2r_free = [None] * 8
                h2s_free = [None] * 4
                f1_free = [lnb_tok, lnb_tok]
                dbg_tok = None
                y_free = [None, None]
                tf_bank_free = {}
                for s in range(NSL):
                    for k in range(4):
                        lt = s * 4 + k
                        ld = dma("sync", sIN, h2s_t(k), h2_hbm[lt],
                                 waits=[h2s_free[k], h2_stores])
                        gb = k % 2
                        bank = ps_tf[:, gb * 1024:(gb + 1) * 1024]
                        t = None
                        for ec in range(8):
                            w = []
                            if ec == 0:
                                w = [ld, tf_bank_free.get(gb)]
                            t = tr(bank[:, ec * 128:(ec + 1) * 128],
                                   h2s_t(k)[:, ec * 128:(ec + 1) * 128], ident[:],
                                   waits=w, inc=(sPE if ec == 7 else None))
                        e = vv("tensor_copy",
                               BIG2[:, 16384:20480].rearrange(
                                   "p (c r) -> p c r", r=512)[:, :, k * 128:(k + 1) * 128],
                               bank[:].rearrange("p (c r) -> p c r", r=128),
                               waits=[(sPE, t)], inc=sDVE)
                        tf_bank_free[gb] = (sDVE, e)
                    h2T_tok = (sDVE, sDVE.n)
                    for ft in range(32):
                        wr = ft % 4
                        ld = dma("sync", sW1, wroll[:, wr * 1024:(wr + 1) * 1024],
                                 w1t_hbm[layer, ft], waits=[w1r_free[wr],
                                                            wt_all_done])
                        par = ft % 2
                        for ec in range(8):
                            w = []
                            if ec == 0:
                                w = [ld, h2T_tok, f1_free[par]]
                            mm(ps_f1[:, par * 512:(par + 1) * 512],
                               wroll[:, wr * 1024 + ec * 128:wr * 1024 + (ec + 1) * 128],
                               h2T_c(ec), ec == 0, ec == 7,
                               waits=w, inc=(sPE if ec == 7 else None))
                        w1r_free[wr] = (sPE, sPE.n)
                        e = act(r1_t(ft), ps_f1[:, par * 512:(par + 1) * 512],
                                AF.Relu, bias=b1T[:, ft:ft + 1],
                                waits=[(sPE, sPE.n)], inc=sACT)
                        f1_free[par] = (sACT, e)
                    r1_tok = (sACT, sACT.n)
                    if DBG_OUT and s == 0:
                        for ft in range(32):
                            dma("sync", sST, dbgB[ft], r1_t(ft), waits=[r1_tok])
                        dma("sync", sST, dbgC[:, :],
                            BIG2[:, 16384:20480], waits=[r1_tok])
                    for hf in range(2):
                        mm_done = [None, None]
                        for fc in range(FF2_NFC):
                            ws_ = fc % 4
                            wr = fc % 8
                            ld = dma("sync", sW1,
                                     wroll[:, 4096 + ws_ * 1024:4096 + (ws_ + 1) * 1024],
                                     w2t_hbm[layer, fc],
                                     waits=[w2s_free[ws_], wt_all_done])
                            hp = vv("tensor_copy",
                                    wroll[:, 8192 + wr * 1024:8192 + (wr + 1) * 1024],
                                    wroll[:, 4096 + ws_ * 1024:4096 + (ws_ + 1) * 1024],
                                    waits=[ld, w2r_free[wr]], inc=sDVE)
                            w2s_free[ws_] = (sDVE, hp)
                            for kk in range(2):
                                k = hf * 2 + kk
                                pary = kk
                                for cc in range(2):
                                    w = []
                                    if fc == 0 and cc == 0:
                                        w = [r1_tok, y_free[pary]]
                                    if cc == 0:
                                        w.append((sDVE, hp))
                                    tk = mm(
                                        ps_y[pary][:, cc * 512:(cc + 1) * 512],
                                        r1_t(fc)[:, k * 128:(k + 1) * 128],
                                        wroll[:, 8192 + wr * 1024 + cc * 512:
                                            8192 + wr * 1024 + (cc + 1) * 512],
                                        fc == 0, fc == FF2_NFC - 1,
                                        waits=w,
                                        inc=(sPE if cc == 1 else None))
                                    if cc == 1:
                                        mm_done[kk] = tk
                            w2r_free[wr] = (sPE, sPE.n)
                        for kk in range(2):
                            k = hf * 2 + kk
                            lt = s * 4 + k
                            pary = kk
                            vv("scalar_tensor_tensor", t1_f[:, :], ps_y[pary][:, :],
                               0.0, LNBC[:, 2 * E:3 * E], ALU.bypass, ALU.add,
                               waits=[(sPE, mm_done[kk]), lnF_tok,
                                      (sACT, sACT.n), dbg_tok], inc=sDVE)
                            yv = vv("tensor_tensor", t1_f[:, :], t1_f[:, :], h2s_t(k),
                                    ALU.add, inc=sDVE)
                            if DBG_OUT:
                                dtk = dma("sync", sST, dbgA[lt], t1_f[:, :],
                                          waits=[(sDVE, yv)])
                                dbg_tok = dtk
                            y_free[pary] = (sDVE, yv)
                            h2s_free[k] = (sDVE, yv)
                            v6, a3 = ln_stats(t1_f[:, :], E, waits=[(sDVE, yv)])
                            u = ln_norm(h1_b, t1_f[:, :], waits=[(sDVE, v6), (sACT, a3)])
                            vv("tensor_tensor", h1_b, h1_b, LNBC[:, 0:E], ALU.mult,
                               waits=[(sACT, u)], inc=sDVE)
                            hv = vv("tensor_tensor", h_t(lt), h1_b, LNBC[:, E:2 * E],
                                    ALU.add, inc=sDVE)
                            h_ready_tok[lt] = (sDVE, hv)
            fence()

        def tile_T_to_big(bigbuf, src_tile, lt, psb, gb, waits, bank_free_d):
            bank = psb[:, gb * 1024:(gb + 1) * 1024]
            t = None
            for ec in range(8):
                w = []
                if ec == 0:
                    w = list(waits) + [bank_free_d.get(gb)]
                t = tr(bank[:, ec * 128:(ec + 1) * 128],
                       src_tile[:, ec * 128:(ec + 1) * 128], ident[:],
                       waits=w, inc=(sPE if ec == 7 else None))
            e = vv("tensor_copy",
                   bigbuf[:, 0:8 * L].rearrange("p (c r) -> p c r", r=L)
                   [:, :, lt * 128:(lt + 1) * 128],
                   bank[:].rearrange("p (c r) -> p c r", r=128),
                   waits=[(sPE, t)], inc=sDVE)
            bank_free_d[gb] = (sDVE, e)
            return e

        with contextlib.ExitStack() as PS:
            psb1 = PS.enter_context(nc.psum_tensor("psb1", [128, 2048], BF16))
            bfree = {}
            b1_free = {0: None, 1: None}
            for i in range(NT):
                par = i % 2
                ld = dma("sync", sIN, xs_t(par), x_in[i * 128:(i + 1) * 128, :],
                         waits=[b1_free[par]])
                v = vv("tensor_tensor", xs_t(par), xs_t(par), h_t(i), ALU.add,
                       waits=[ld, h_ready_tok[i]], inc=sDVE)
                dst = out[i * 128:(i + 1) * 128, :] if stage == 1 else x_a[i]
                st = dma("sync", sST, dst, xs_t(par), waits=[(sDVE, v)])
                b1_free[par] = st
                if stage > 1:
                    v6, a3 = ln_stats(xs_t(par), E, waits=[(sDVE, v)])
                    u = ln_norm(NTR[:, 0:E], xs_t(par),
                                waits=[(sDVE, v6), (sACT, a3), (sDVE, sDVE.n),
                                       (sPE, sPE.n)])
                    e = tile_T_to_big(BIG2, NTR[:, 0:E], i, psb1, i % 2,
                                      [(sACT, u)], bfree)
                    b1_free[par] = (sACT, u)
        fence()

        def moe_call(call, ST):
            cwb = ST["cwb"]
            gT_c = ST["gT_c"]
            with contextlib.ExitStack() as PS:
                ps_r = PS.enter_context(nc.psum_tensor("psr_%d" % call, [128, 512], F32))
                ps_t2 = PS.enter_context(nc.psum_tensor("pst2_%d" % call, [128, 512], F32))
                gw8 = PS.enter_context(nc.sbuf_tensor("gw8_%d" % call, [8, E], F32))
                gwT = PS.enter_context(nc.sbuf_tensor("gwT_%d" % call, [128, 64], BF16))
                cwrow = PS.enter_context(nc.sbuf_tensor("cwr_%d" % call, [4, 128], F32))
                ld = dma("sync", sIN, gw8[:, :], gate_w)
                t = None
                for ec in range(8):
                    t = tr(ps_t2[:, ec * 8:(ec + 1) * 8], gw8[0:8, ec * 128:(ec + 1) * 128],
                           identf[0:8, 0:8], waits=[ld] if ec == 0 else [],
                           inc=(sPE if ec == 7 else None))
                gtok = vv("tensor_copy", gwT[:, 0:64], ps_t2[:, 0:64],
                          waits=[(sPE, t)], inc=sDVE)
                cw_done = None
                for lt in range(NT):
                    for ec in range(8):
                        mm(ps_r[:, 0:8], kt_c(ec)[:, lt * 128:(lt + 1) * 128],
                           gwT[:, ec * 8:(ec + 1) * 8], ec == 0, ec == 7,
                           waits=([(sDVE, gtok), cw_done] if ec == 0 else []),
                           inc=(sPE if ec == 7 else None))
                    mmtok = sPE.n
                    v1 = vv("reduce_max", stat[:, 0:1], ps_r[:, 0:4], axis=AX.X,
                            waits=[(sPE, mmtok)], inc=sDVE)
                    v2 = vv("tensor_scalar", stat[:, 1:2], stat[:, 0:1], -1.0, None,
                            ALU.mult, waits=[(sDVE, v1)], inc=sDVE)
                    aw = act(stat[:, 2:6], ps_r[:, 0:4], AF.Exp, bias=stat[:, 1:2],
                             waits=[(sDVE, v2)], inc=sACT)
                    w0, w1_, w2_, w3_ = (stat[:, 2:3], stat[:, 3:4], stat[:, 4:5],
                                         stat[:, 5:6])
                    va = vv("tensor_tensor", stat[:, 6:7], w0, w1_, ALU.max,
                            waits=[(sACT, aw)], inc=sDVE)
                    vb = vv("tensor_tensor", stat[:, 7:8], w0, w1_, ALU.min,
                            waits=[(sDVE, va)], inc=sDVE)
                    vc = vv("tensor_tensor", stat[:, 8:9], w2_, w3_, ALU.max,
                            waits=[(sDVE, vb)], inc=sDVE)
                    vd = vv("tensor_tensor", stat[:, 9:10], w2_, w3_, ALU.min,
                            waits=[(sDVE, vc)], inc=sDVE)
                    vm1 = vv("tensor_tensor", stat[:, 10:11], stat[:, 6:7],
                             stat[:, 8:9], ALU.max, waits=[(sDVE, vd)], inc=sDVE)
                    vmn = vv("tensor_tensor", stat[:, 11:12], stat[:, 6:7],
                             stat[:, 8:9], ALU.min, waits=[(sDVE, vm1)], inc=sDVE)
                    vsel = vv("tensor_tensor", stat[:, 12:13], stat[:, 6:7],
                              stat[:, 8:9], ALU.is_ge, waits=[(sDVE, vmn)], inc=sDVE)
                    vt1 = vv("tensor_tensor", stat[:, 13:14], stat[:, 12:13],
                             stat[:, 7:8], ALU.mult, waits=[(sDVE, vsel)], inc=sDVE)
                    vsm = vv("tensor_scalar", stat[:, 12:13], stat[:, 12:13], -1.0,
                             1.0, ALU.mult, ALU.add, waits=[(sDVE, vt1)], inc=sDVE)
                    vt2 = vv("tensor_tensor", stat[:, 14:15], stat[:, 12:13],
                             stat[:, 9:10], ALU.mult, waits=[(sDVE, vsm)], inc=sDVE)
                    vt3 = vv("tensor_tensor", stat[:, 13:14], stat[:, 13:14],
                             stat[:, 14:15], ALU.add, waits=[(sDVE, vt2)], inc=sDVE)
                    vm2 = vv("tensor_tensor", stat[:, 11:12], stat[:, 11:12],
                             stat[:, 13:14], ALU.max, waits=[(sDVE, vt3)], inc=sDVE)
                    vms = vv("tensor_tensor", stat[:, 14:15], stat[:, 10:11],
                             stat[:, 11:12], ALU.add, waits=[(sDVE, vm2)], inc=sDVE)
                    vrq = vv("reciprocal", stat[:, 15:16], stat[:, 14:15],
                             waits=[(sDVE, vms)], inc=sDVE)
                    vind = vv("tensor_scalar", UNA[:, 0:4], stat[:, 2:6],
                              stat[:, 11:12], None, ALU.is_ge,
                              waits=[(sDVE, vrq)], inc=sDVE)
                    vcw = vv("scalar_tensor_tensor", stat[:, 2:6], stat[:, 2:6],
                             stat[:, 15:16], UNA[:, 0:4], ALU.mult, ALU.mult,
                             waits=[(sDVE, vind)], inc=sDVE)
                    t = tr(ps_t2[0:4, 128:256], stat[:, 2:6], identf[:],
                           waits=[(sDVE, vcw)], inc=sPE)
                    ce = vv("tensor_copy", cwrow[0:4, :], ps_t2[0:4, 128:256],
                            waits=[(sPE, t)], inc=sDVE)
                    cst = dma("sync", sST, cwt_hbm[:, lt * 128:(lt + 1) * 128],
                              cwrow[0:4, :], waits=[(sDVE, ce)])
                    cw_done = cst
                cw_all = sST.all()
            fence()
            dtk = act(stat[:, 0:1], consts[:, 2:3], AF.Silu, inc=sACT)
            wait_only("tensor", [(sACT, dtk)])
            y_sem_guard = [None] * 8
            for e_i in range(NEXP):
                for s in range(L // 1024):
                    with contextlib.ExitStack() as P2:
                        ps_c = P2.enter_context(
                            nc.psum_tensor("psc_%d_%d_%d" % (call, e_i, s),
                                           [128, 1024], F32))
                        crow = P2.enter_context(
                            nc.sbuf_tensor("crow_%d_%d_%d" % (call, e_i, s),
                                           [1, 1024], BF16))
                        crow2 = P2.enter_context(
                            nc.sbuf_tensor("crow2_%d_%d_%d" % (call, e_i, s),
                                           [1, 1024], BF16))
                        ld = dma("gpsimd", sWG, crow[0:1, :],
                                 cwt_hbm[e_i, s * 1024:(s + 1) * 1024][None, :],
                                 waits=cw_all + [ST["cwb_free"]])
                        hp = vv("tensor_copy", crow2[0:1, :], crow[0:1, :],
                                waits=[ld], inc=sDVE)
                        for hh in range(2):
                            mm(ps_c[:, hh * 512:(hh + 1) * 512], ones_r[0:1, :],
                               crow2[0:1, hh * 512:(hh + 1) * 512], True, True,
                               waits=[(sDVE, hp), ST["cwb_free"]] if hh == 0 else [],
                               inc=(sPE if hh == 1 else None))
                        cwb_tok = vv("tensor_copy", cwb[:, :], ps_c[:, :],
                                     waits=[(sPE, sPE.n)], inc=sDVE)
                        ST["cwb_tok"] = (sDVE, cwb_tok)
                    with contextlib.ExitStack() as P2:
                        psA = P2.enter_context(
                            nc.psum_tensor("psA_%d_%d_%d" % (call, e_i, s),
                                           [128, 4096], F32))
                        w13 = P2.enter_context(
                            nc.sbuf_tensor("w13_%d_%d_%d" % (call, e_i, s),
                                           [128, 8192], BF16))
                        w1_free = {0: None, 1: None}
                        w3_free = {0: None, 1: None}
                        psA_free = {0: None, 1: None}
                        for k in range(44):
                            par = k % 2
                            ld1 = dma("gpsimd", sWG,
                                      w13[:, par * 1024:(par + 1) * 1024]
                                      .rearrange("p (c j) -> p c j", j=128),
                                      mw1[e_i, :, k * 128:(k + 1) * 128]
                                      .rearrange("(c p) j -> p c j", p=128),
                                      waits=[w1_free[par]] +
                                            ([ST["cwb_tok"]] if k < 2 else []))
                            ld3 = dma("gpsimd", sWG,
                                      w13[:, 2048 + par * 1024:2048 + (par + 1) * 1024]
                                      .rearrange("p (c j) -> p c j", j=128),
                                      mw3[e_i, :, k * 128:(k + 1) * 128]
                                      .rearrange("(c p) j -> p c j", p=128),
                                      waits=[w3_free[par]])
                            pa1 = psA[:, par * 1024:(par + 1) * 1024]
                            pa3 = psA[:, 2048 + par * 1024:2048 + (par + 1) * 1024]
                            for hh in range(2):
                                for dc in range(8):
                                    w = []
                                    if hh == 0 and dc == 0:
                                        w = [ld1, psA_free[par], ST["cwb_tok"]]
                                    mm(pa1[:, hh * 512:(hh + 1) * 512],
                                       w13[:, par * 1024 + dc * 128:par * 1024 + (dc + 1) * 128],
                                       kt_c(dc)[:, s * 1024 + hh * 512:s * 1024 + hh * 512 + 512],
                                       dc == 0, dc == 7,
                                       waits=w, inc=(sPE if dc == 7 else None))
                            w1_free[par] = (sPE, sPE.n)
                            for hh in range(2):
                                for dc in range(8):
                                    w = []
                                    if hh == 0 and dc == 0:
                                        w = [ld3]
                                    mm(pa3[:, hh * 512:(hh + 1) * 512],
                                       w13[:, 2048 + par * 1024 + dc * 128:
                                           2048 + par * 1024 + (dc + 1) * 128],
                                       kt_c(dc)[:, s * 1024 + hh * 512:s * 1024 + hh * 512 + 512],
                                       dc == 0, dc == 7,
                                       waits=w, inc=(sPE if dc == 7 else None))
                            w3_free[par] = (sPE, sPE.n)
                            a1 = act(UNA_BF[:, 0:1024], pa1, AF.Silu,
                                     waits=[(sPE, sPE.n)], inc=sACT)
                            gv = vv("tensor_tensor", gT_c(k), UNA_BF[:, 0:1024],
                                    pa3, ALU.mult, waits=[(sACT, a1)], inc=sDVE)
                            psA_free[par] = (sDVE, gv)
                        gT_tok = (sDVE, sDVE.n)
                    with contextlib.ExitStack() as P2:
                        psB = P2.enter_context(
                            nc.psum_tensor("psB_%d_%d_%d" % (call, e_i, s),
                                           [128, 4096], F32))
                        w2c = P2.enter_context(
                            nc.sbuf_tensor("w2c_%d_%d_%d" % (call, e_i, s),
                                           [128, 4096], BF16))
                        w2_free = [None] * 4
                        prev_evac = None
                        for th in range(2):
                            mmB = None
                            for k in range(44):
                                wr = k % 4
                                ldw = dma("gpsimd", sWG,
                                          w2c[:, wr * 1024:(wr + 1) * 1024],
                                          mw2[e_i, k * 128:(k + 1) * 128, :],
                                          waits=[w2_free[wr]] +
                                                ([gT_tok] if th == 0 and k < 4
                                                 else []))
                                for dc in range(8):
                                    w = []
                                    if k == 0 and dc == 0:
                                        w = ([gT_tok, ST["psB_free"],
                                              ST["cwb_tok"]] if th == 0
                                             else [prev_evac])
                                    if dc == 0:
                                        w.append(ldw)
                                    mmB = mm(psB[:, dc * 512:(dc + 1) * 512],
                                             w2c[:, (k % 4) * 1024 + dc * 128:
                                                 (k % 4) * 1024 + (dc + 1) * 128],
                                             gT_c(k)[:, th * 512:(th + 1) * 512],
                                             k == 0, k == 43,
                                             waits=w, inc=(sPE if dc == 7 else None))
                                w2_free[wr] = (sPE, sPE.n)
                            evs = []
                            for dc in range(8):
                                sl_ = dc % 2
                                yv2 = vv("tensor_tensor",
                                         UNA[:, 512 + sl_ * 512:1024 + sl_ * 512],
                                         psB[:, dc * 512:(dc + 1) * 512],
                                         cwb[:, th * 512:(th + 1) * 512], ALU.mult,
                                         waits=[(sPE, mmB), ST["ystg"][sl_]],
                                         inc=sDVE)
                                yst = dma("gpsimd", sSY,
                                          y_hbm[dc, :, s * 1024 + th * 512:
                                                s * 1024 + th * 512 + 512],
                                          UNA[:, 512 + sl_ * 512:1024 + sl_ * 512],
                                          waits=[(sDVE, yv2), y_sem_guard[dc]],
                                          accum=(e_i > 0))
                                y_sem_guard[dc] = yst
                                ST["ystg"][sl_] = yst
                                evs.append((sDVE, yv2))
                            prev_evac = evs[-1]
                            ST["psB_free"] = evs[-1]
                            ST["cwb_free"] = evs[-1]
            return None


        cwt_hbm = nc.dram_tensor("cwt_scr", [4, L], F32, kind=dbgkind).ap()

        def mk_gT_c(big3m):
            def gT_c(k):
                if k < 32:
                    return BIG1[:, k * 1024:(k + 1) * 1024]
                return big3m[:, (k - 32) * 1024:(k - 31) * 1024]
            return gT_c

        if stage >= 2:
            with contextlib.ExitStack() as MS:
                BIG3_m = MS.enter_context(nc.sbuf_tensor("BIG3m", [128, 12288], BF16))
                cwb = MS.enter_context(nc.sbuf_tensor("cwb_m", [128, 1024], BF16))
                ST = {"w2_ld": {}, "cwb_free": None, "cwb_tok": None,
                      "ystg": {0: None, 1: None}, "psB_free": None,
                      "cwb": cwb, "gT_c": mk_gT_c(BIG3_m)}
                moe_call(0, ST)
            fence()
            dtk = act(stat[:, 0:1], consts[:, 2:3], AF.Exp, inc=sACT)
            wait_only("tensor", [(sACT, dtk)])

        if stage >= 2:
            with contextlib.ExitStack() as PS:
                psf = PS.enter_context(nc.psum_tensor("psb2f", [128, 1024], F32))
                psb2 = PS.enter_context(nc.psum_tensor("psb2b", [128, 2048], BF16))
                ysl = PS.enter_context(nc.sbuf_tensor("ysl", [128, 1024], F32))
                ytr_free = {0: None, 1: None}
                bfree2 = {}
                b2_free = {0: None, 1: None}
                yld_free = None
                for i in range(NT):
                    par = i % 2
                    ld = dma("sync", sIN, xs_t(par), x_a[i], waits=[b2_free[par]])
                    yld = dma("sync", sIN,
                              ysl[:].rearrange("p (c j) -> p c j", j=128),
                              y_hbm[:, :, i * 128:(i + 1) * 128]
                              .rearrange("c p j -> p c j"),
                              waits=[yld_free])
                    vtok = None
                    for dc in range(8):
                        gb = dc % 2
                        t = tr(psf[:, gb * 512:gb * 512 + 128],
                               ysl[:, dc * 128:(dc + 1) * 128], identf[:],
                               waits=([yld] if dc == 0 else []) + [ytr_free.get(gb)],
                               inc=sPE)
                        vtok = vv("tensor_tensor", xs_t(par)[:, dc * 128:(dc + 1) * 128],
                                  xs_t(par)[:, dc * 128:(dc + 1) * 128],
                                  psf[:, gb * 512:gb * 512 + 128], ALU.add,
                                  waits=[(sPE, t), ld], inc=sDVE)
                        ytr_free[gb] = (sDVE, vtok)
                    yld_free = (sPE, sPE.n)
                    dst = out[i * 128:(i + 1) * 128, :] if stage == 2 else x_b[i]
                    st = dma("sync", sST, dst, xs_t(par), waits=[(sDVE, vtok)])
                    b2_free[par] = st
                    if stage > 2:
                        v6, a3 = ln_stats(xs_t(par), E, waits=[(sDVE, vtok)])
                        u = ln_norm(NTR[:, 0:E], xs_t(par),
                                    waits=[(sDVE, v6), (sACT, a3), (sDVE, sDVE.n),
                                           (sPE, sPE.n)])
                        tile_T_to_big(BIG1, NTR[:, 0:E], i, psb2, i % 2,
                                      [(sACT, u)], bfree2)
                        b2_free[par] = (sACT, u)
            fence()

        def emit_attn_b():
            with contextlib.ExitStack() as AS:
                BIG3b = AS.enter_context(nc.sbuf_tensor("BIG3b", [128, 2 * L + NT * 130], BF16))
                attnT = AS.enter_context(nc.sbuf_tensor("attnT", [128, L], BF16))
                sT_sb = AS.enter_context(nc.sbuf_tensor("sT_sb", [128, 2048], BF16))
                rden = AS.enter_context(nc.sbuf_tensor("rden", [1, 1024], BF16))
                rdenf = AS.enter_context(nc.sbuf_tensor("rdenf", [1, 1024], F32))
                wstag = AS.enter_context(nc.sbuf_tensor("wstb", [128, 1024], BF16))
                browf = AS.enter_context(nc.sbuf_tensor("brwf", [1, E], BF16))
                browf2 = AS.enter_context(nc.sbuf_tensor("brwf2", [1, E], BF16))
                bvb = AS.enter_context(nc.sbuf_tensor("bvb", [128, 128], BF16))

                def bwT(i):
                    return BIG2[:, i * 8192:(i + 1) * 8192]

                with contextlib.ExitStack() as PS:
                    ps_tp = PS.enter_context(nc.psum_tensor("pstb", [128, 2048], BF16))
                    ps_bc = PS.enter_context(nc.psum_tensor("psbb", [128, 1024], F32))
                    tpb_free = {}
                    wstg_free = None
                    for wi, wsrc in enumerate((bW["q"], bW["k"], bW["v"], bW["o"])):
                        for dc in range(8):
                            gb = dc % 2
                            ld = dma("gpsimd", sWG, wstag[:, :],
                                     wsrc[dc * 128:(dc + 1) * 128, :],
                                     waits=[wstg_free])
                            bank = ps_tp[:, gb * 1024:(gb + 1) * 1024]
                            t = None
                            for ec in range(8):
                                w = []
                                if ec == 0:
                                    w = [ld, tpb_free.get(gb)]
                                t = tr(bank[:, ec * 128:(ec + 1) * 128],
                                       wstag[:, ec * 128:(ec + 1) * 128], ident[:],
                                       waits=w, inc=(sPE if ec == 7 else None))
                            e = vv("tensor_copy",
                                   bwT(wi)[:].rearrange("p (c r) -> p c r", r=1024)
                                   [:, :, dc * 128:(dc + 1) * 128],
                                   bank[:].rearrange("p (c r) -> p c r", r=128),
                                   waits=[(sPE, t)], inc=sDVE)
                            tpb_free[gb] = (sDVE, e)
                            wstg_free = (sPE, sPE.n)
                    btok = None
                    for bi, (bsrc, bdst, pcol) in enumerate(
                            ((bb["q"], bqT, 0), (bb["k"], bkT, 0), (bb["v"], b1T, 0))):
                        ld = dma("sync", sIN, row8[0:8, :],
                                 bsrc.rearrange("(c j) -> c j", j=128), waits=[btok])
                        t = tr(ps_bc[:, 0:8], row8[0:8, :], identf[0:8, 0:8],
                               waits=[ld, btok], inc=sPE)
                        e = vv("tensor_copy", bdst[:, 0:8], ps_bc[:, 0:8],
                               waits=[(sPE, t)], inc=sDVE)
                        btok = (sDVE, e)
                    ld = dma("gpsimd", sWG, browf[0:1, :], bb["o"][None, :],
                             waits=[btok])
                    bo_tok = bcast_row(LNBC[:, 0:E], browf, browf2, E, ps_bc,
                                       waits=[ld], pe_waits=[btok])
                    vones = vv("memset",
                               BIG3b[:, 2 * L:2 * L + NT * 130]
                               .rearrange("p (a b) -> p a b", b=65)
                               [:, :, 64:65], 1.0, inc=sDVE)
                fence()

                at_stores = []
                for pr in range(8):
                    with contextlib.ExitStack() as PS:
                        ps_p = PS.enter_context(
                            nc.psum_tensor("psp_%d" % pr, [128, 1024], F32))
                        ps_v = PS.enter_context(
                            nc.psum_tensor("psv_%d" % pr, [128, 1024], F32))
                        ps_b2 = PS.enter_context(
                            nc.psum_tensor("psb2_%d" % pr, [128, 512], F32))
                        ld = dma("gpsimd", sWG, browf[0:1, 0:128],
                                 bb["v"][pr * 128:(pr + 1) * 128][None, :],
                                 waits=[(sDVE, sDVE.n)])
                        hp = vv("tensor_copy", browf2[0:1, 0:128], browf[0:1, 0:128],
                                waits=[ld], inc=sDVE)
                        t = mm(ps_b2[:, 0:128], ones_r[0:1, :], browf2[0:1, 0:128],
                               True, True, waits=[(sDVE, hp), (sDVE, sDVE.n),
                                                  (sACT, sACT.n)], inc=sPE)
                        bvb_tok = vv("tensor_copy", bvb[:, :], ps_b2[:, 0:128],
                                     waits=[(sPE, sPE.n)], inc=sDVE)
                        pp_free = {0: None, 1: None}
                        for which, (wq_i, bcol, dstoff) in enumerate(
                                ((0, bqT, 0), (1, bkT, L))):
                            for s in range(NSL):
                                par = s % 2
                                pk = ps_p[:, par * 512:(par + 1) * 512]
                                for ec in range(8):
                                    w = []
                                    if ec == 0:
                                        w = [pp_free[par], (sDVE, sDVE.n)]
                                    mm(pk, bwT(wq_i)[:, ec * 1024 + pr * 128:
                                                     ec * 1024 + (pr + 1) * 128],
                                       BIG1[:, ec * L + s * 512:ec * L + (s + 1) * 512],
                                       ec == 0, ec == 7,
                                       waits=w, inc=(sPE if ec == 7 else None))
                                e = act(BIG3b[:, dstoff + s * 512:dstoff + (s + 1) * 512],
                                        pk, AF.Exp, bias=bcol[:, pr:pr + 1],
                                        waits=[(sPE, sPE.n)], inc=sACT)
                                pp_free[par] = (sACT, e)
                        psv_free = {0: None, 1: None}
                        for lt in range(NT):
                            par = lt % 2
                            pv = ps_v[:, par * 512:par * 512 + 128]
                            for ec in range(8):
                                w = []
                                if ec == 0:
                                    w = [psv_free[par], (sDVE, bvb_tok)]
                                mm(pv, BIG1[:, ec * L + lt * 128:ec * L + (lt + 1) * 128],
                                   bwT(2)[:, ec * 1024 + pr * 128:ec * 1024 + (pr + 1) * 128],
                                   ec == 0, ec == 7,
                                   waits=w, inc=(sPE if ec == 7 else None))
                            e1 = vv("tensor_tensor",
                                    BIG3b[:, 2 * L + lt * 130:2 * L + lt * 130 + 64],
                                    pv[:, 0:64], bvb[:, 0:64], ALU.add,
                                    waits=[(sPE, sPE.n)], inc=sDVE)
                            e2 = vv("tensor_tensor",
                                    BIG3b[:, 2 * L + lt * 130 + 65:2 * L + lt * 130 + 129],
                                    pv[:, 64:128], bvb[:, 64:128], ALU.add,
                                    inc=sDVE)
                            psv_free[par] = (sDVE, e2)
                    fence(("tensor", "scalar", "vector"))
                    with contextlib.ExitStack() as PS:
                        ps_sT = PS.enter_context(
                            nc.psum_tensor("pss_%d" % pr, [128, 2048], F32))
                        ps_num = PS.enter_context(
                            nc.psum_tensor("psn_%d" % pr, [128, 1024], F32))
                        ps_bc = PS.enter_context(
                            nc.psum_tensor("psk_%d" % pr, [128, 1024], F32))
                        sT_free = {0: None, 1: None}
                        num_guard = None
                        for half in range(2):
                            qrow = half * 64
                            for lc in range(L // 1024):
                                ntc = lc * 8 + 8
                                last_num = None
                                for tc in range(ntc):
                                    par = tc % 2
                                    pst = ps_sT[:, par * 1024:(par + 1) * 1024]
                                    for cc in range(2):
                                        w = []
                                        if cc == 0:
                                            w = [sT_free[par]]
                                            if half == 0 and lc == 0 and tc == 0:
                                                w.append(num_guard)
                                        mm(pst[:, cc * 512:(cc + 1) * 512],
                                           BIG3b[qrow:qrow + 64, L + tc * 128:
                                                 L + (tc + 1) * 128],
                                           BIG3b[qrow:qrow + 64,
                                                 lc * 1024 + cc * 512:
                                                 lc * 1024 + (cc + 1) * 512],
                                           True, True,
                                           waits=w, inc=(sPE if cc == 1 else None))
                                    mmtok = sPE.n
                                    cp = vv("tensor_copy", sT_sb[:, par * 1024:
                                                                 (par + 1) * 1024],
                                            pst, waits=[(sPE, mmtok)], inc=sDVE)
                                    if tc >= lc * 8:
                                        off = (tc - lc * 8) * 128
                                        if off > 0:
                                            cp = vv("memset",
                                                    sT_sb[:, par * 1024:par * 1024 + off],
                                                    0.0, waits=[(sDVE, cp)], inc=sDVE)
                                        cp = vv("tensor_tensor",
                                                sT_sb[:, par * 1024 + off:
                                                      par * 1024 + off + 128],
                                                sT_sb[:, par * 1024 + off:
                                                      par * 1024 + off + 128],
                                                tri[:], ALU.mult,
                                                waits=[(sDVE, cp)], inc=sDVE)
                                    for cc in range(2):
                                        w = []
                                        if cc == 0:
                                            w = [(sDVE, cp)]
                                            if tc == 0:
                                                w.append(num_guard)
                                        last_num = mm(
                                            ps_num[0:65, cc * 512:(cc + 1) * 512],
                                            BIG3b[:, 2 * L + tc * 130 + half * 65:
                                                  2 * L + tc * 130 + half * 65 + 65],
                                            sT_sb[:, par * 1024 + cc * 512:
                                                  par * 1024 + (cc + 1) * 512],
                                            tc == 0, tc == ntc - 1,
                                            waits=w,
                                            inc=(sPE if (tc == ntc - 1 and cc == 1)
                                                 else None))
                                    sT_free[par] = (sPE, sPE.n)
                                rc = vv("reciprocal", rdenf[0:1, :], ps_num[64:65, :],
                                        waits=[(sPE, last_num)], inc=sDVE)
                                rc = vv("tensor_copy", rden[0:1, :], rdenf[0:1, :],
                                        waits=[(sDVE, rc)], inc=sDVE)
                                for cc in range(2):
                                    mm(ps_bc[:, cc * 512:(cc + 1) * 512],
                                       ones_r[0:1, :],
                                       rden[0:1, cc * 512:(cc + 1) * 512],
                                       True, True,
                                       waits=[(sDVE, rc), num_guard] if cc == 0 else [],
                                       inc=(sPE if cc == 1 else None))
                                bcb = act(NTR[0:64, 0:1024], ps_bc[0:64, :], AF.Identity,
                                          waits=[(sPE, sPE.n)], inc=sACT)
                                ng = vv("tensor_tensor",
                                        attnT[qrow:qrow + 64, lc * 1024:(lc + 1) * 1024],
                                        ps_num[0:64, :], NTR[0:64, 0:1024], ALU.mult,
                                        waits=[(sACT, bcb)], inc=sDVE)
                                num_guard = (sDVE, ng)
                        st = dma("sync", sST,
                                 at_hbm[pr].rearrange("c p j -> p c j"),
                                 attnT[:].rearrange("p (c j) -> p c j", j=128),
                                 waits=[num_guard])
                        at_stores.append(st)
                        num_guard = st
                    fence(("tensor", "scalar", "vector"))
                with contextlib.ExitStack() as PS:
                    ps_o = PS.enter_context(nc.psum_tensor("pso", [128, 2048], F32))
                    atroll = PS.enter_context(nc.sbuf_tensor("atrl", [128, 2048], BF16))
                    ar_free = {0: None, 1: None}
                    o_free = {0: None, 1: None}
                    b3_free = {0: None, 1: None}
                    for i in range(NT):
                        par = i % 2
                        ld = dma("sync", sIN,
                                 atroll[:, par * 1024:(par + 1) * 1024]
                                 .rearrange("p (c j) -> p c j", j=128),
                                 at_hbm[:, i].rearrange("c p j -> p c j"),
                                 waits=[ar_free[par]] + at_stores)
                        po = ps_o[:, par * 1024:(par + 1) * 1024]
                        for pr in range(8):
                            for cc in range(2):
                                w = []
                                if pr == 0 and cc == 0:
                                    w = [ld, o_free[par]]
                                mm(po[:, cc * 512:(cc + 1) * 512],
                                   atroll[:, par * 1024 + pr * 128:
                                          par * 1024 + (pr + 1) * 128],
                                   bwT(3)[:, pr * 1024 + cc * 512:
                                          pr * 1024 + (cc + 1) * 512],
                                   pr == 0, pr == 7,
                                   waits=w, inc=(sPE if (pr == 7 and cc == 1) else None))
                        ar_free[par] = (sPE, sPE.n)
                        xld = dma("sync", sIN, xs_t(par), x_b[i], waits=[b3_free[par]])
                        v1 = vv("tensor_tensor", xs_t(par), xs_t(par), po, ALU.add,
                                waits=[(sPE, sPE.n), xld], inc=sDVE)
                        v2 = vv("tensor_tensor", xs_t(par), xs_t(par), LNBC[:, 0:E],
                                ALU.add, inc=sDVE)
                        o_free[par] = (sDVE, v2)
                        dst = out[i * 128:(i + 1) * 128, :] if stage == 3 else x_a[i]
                        st = dma("sync", sST, dst, xs_t(par), waits=[(sDVE, v2)])
                        b3_free[par] = st
                        if stage > 3:
                            v6, a3 = ln_stats(xs_t(par), E, waits=[(sDVE, v2)])
                            u = ln_norm(h_t(i), xs_t(par),
                                        waits=[(sDVE, v6), (sACT, a3), h_u_guard[i]])
                            h_ready_tok[i] = (sACT, u)
                            b3_free[par] = (sACT, u)
                if stage > 3:
                    with contextlib.ExitStack() as PS:
                        psb3 = PS.enter_context(nc.psum_tensor("psb3", [128, 2048], BF16))
                        bfree3 = {}
                        for i in range(NT):
                            tile_T_to_big(BIG2, h_t(i), i, psb3, i % 2,
                                          [h_ready_tok[i]], bfree3)

        if stage >= 3:
            h_u_guard = [None] * NT
            emit_attn_b()
            fence()
            if stage >= 4:
                with contextlib.ExitStack() as MS:
                    BIG3_m = MS.enter_context(nc.sbuf_tensor("BIG3m2", [128, 12288], BF16))
                    cwb = MS.enter_context(nc.sbuf_tensor("cwb_m2", [128, 1024], BF16))
                    ST = {"w2_ld": {}, "cwb_free": None, "cwb_tok": None,
                          "ystg": {0: None, 1: None}, "psB_free": None,
                          "cwb": cwb, "gT_c": mk_gT_c(BIG3_m)}
                    moe_call(1, ST)
                fence()
                with contextlib.ExitStack() as PS:
                    psf = PS.enter_context(nc.psum_tensor("psb4f", [128, 1024], F32))
                    ysl = PS.enter_context(nc.sbuf_tensor("ysl4", [128, 1024], F32))
                    ytr_free = {0: None, 1: None}
                    b4_free = {0: None, 1: None}
                    yld_free = None
                    for i in range(NT):
                        par = i % 2
                        ld = dma("sync", sIN, xs_t(par), x_a[i], waits=[b4_free[par]])
                        yld = dma("sync", sIN,
                                  ysl[:].rearrange("p (c j) -> p c j", j=128),
                                  y_hbm[:, :, i * 128:(i + 1) * 128]
                                  .rearrange("c p j -> p c j"),
                                  waits=[yld_free])
                        vtok = None
                        for dc in range(8):
                            gb = dc % 2
                            t = tr(psf[:, gb * 512:gb * 512 + 128],
                                   ysl[:, dc * 128:(dc + 1) * 128], identf[:],
                                   waits=([yld] if dc == 0 else []) + [ytr_free.get(gb)],
                                   inc=sPE)
                            vtok = vv("tensor_tensor",
                                      xs_t(par)[:, dc * 128:(dc + 1) * 128],
                                      xs_t(par)[:, dc * 128:(dc + 1) * 128],
                                      psf[:, gb * 512:gb * 512 + 128], ALU.add,
                                      waits=[(sPE, t), ld], inc=sDVE)
                            ytr_free[gb] = (sDVE, vtok)
                        yld_free = (sPE, sPE.n)
                        st = dma("sync", sST, out[i * 128:(i + 1) * 128, :],
                                 xs_t(par), waits=[(sDVE, vtok)])
                        b4_free[par] = st

        wait_only("sync", sST.all() + sSY.all())

        blk = ES.enter_context(nc.Block())
        blk.tensor(lambda t: [f(t) for f in R.ops["tensor"]])
        blk.scalar(lambda s2: [f(s2) for f in R.ops["scalar"]])
        blk.vector(lambda v: [f(v) for f in R.ops["vector"]])
        blk.gpsimd(lambda g: [f(g) for f in R.ops["gpsimd"]])
        blk.sync(lambda y: [f(y) for f in R.ops["sync"]])

    return nc


_CACHE = {}


def kernel(**inputs):
    if "nc" not in _CACHE:
        _CACHE["nc"] = build_program(STAGE)
    nc = _CACHE["nc"]
    x = np.ascontiguousarray(np.asarray(inputs["L0_attn_a_x"], dtype=np.float32))
    base = {
        "aWq": inputs["L0_attn_a_Wq"], "aWk": inputs["L0_attn_a_Wk"],
        "abq": inputs["L0_attn_a_bq"], "abk": inputs["L0_attn_a_bk"],
        "aln1_w": inputs["L0_attn_a_ln1_w"], "aln1_b": inputs["L0_attn_a_ln1_b"],
        "aln2_w": inputs["L0_attn_a_ln2_w"], "aln2_b": inputs["L0_attn_a_ln2_b"],
        "aln3_w": inputs["L0_attn_a_ln3_w"], "aln3_b": inputs["L0_attn_a_ln3_b"],
        "aW1": inputs["L0_attn_a_W1"], "ab1": inputs["L0_attn_a_b1"],
        "aW2": inputs["L0_attn_a_W2"], "ab2": inputs["L0_attn_a_b2"],
        "bWq": inputs["L0_attn_b_Wq"], "bbq": inputs["L0_attn_b_bq"],
        "bWk": inputs["L0_attn_b_Wk"], "bbk": inputs["L0_attn_b_bk"],
        "bWv": inputs["L0_attn_b_Wv"], "bbv": inputs["L0_attn_b_bv"],
        "bWo": inputs["L0_attn_b_Wo"], "bbo": inputs["L0_attn_b_bo"],
        "gate_w": inputs["L0_ffn_gate_w"], "mw1": inputs["L0_ffn_w1"],
        "mw2": inputs["L0_ffn_w2"], "mw3": inputs["L0_ffn_w3"],
    }
    base = {k: np.ascontiguousarray(np.asarray(v, dtype=np.float32))
            for k, v in base.items()}
    in_maps = []
    for c in range(8):
        m = dict(base)
        m["x"] = np.ascontiguousarray(x[c])
        in_maps.append(m)
    res = run_bass_kernel_spmd(nc, in_maps, core_ids=list(range(8)))
    return np.stack([np.asarray(r["out"], dtype=np.float32) for r in res.results],
                    axis=0)
